# version 12 (final)
# baseline (speedup 1.0000x reference)
import contextlib
import os as _os
import numpy as np
_KPH = int(_os.environ.get("KPH", "99"))
_KQC = int(_os.environ.get("KQC", "4"))
_KWOUT = int(_os.environ.get("KWOUT", "1"))
_KH = int(_os.environ.get("KH", "16"))
_KWL = int(_os.environ.get("KWL", "3"))

import concourse.bass as bass
import concourse.mybir as mybir
from concourse.bass_isa import ReduceOp
from concourse.bass_utils import run_bass_kernel_spmd

F32 = mybir.dt.float32
BF16 = mybir.dt.bfloat16
FP8 = mybir.dt.float8e4
AF = mybir.ActivationFunctionType
ALU = mybir.AluOpType
AXX = mybir.AxisListType.X

S, DM, H, HD, NF = 2048, 2048, 16, 128, 8192
KTOP = 819
TT, DC, NCH = 16, 16, 64
ESC = 0.08838834764831845
C30 = -30.0
SHIFT = 32.0
LNEPS = 1e-5
NGRP = 8
PCOEF = [1.28155 + 0.035, 1.28155 + 0.10, 1.28155 - 0.03]
NIT = 7
_DBG = bool(_os.environ.get("KDBG"))
KTGT = KTOP - 8.5


def build_program():
    nc = bass.Bass("TRN2", target_bir_lowering=False)

    x_h = nc.dram_tensor("x", [S, DM], F32, kind="ExternalInput").ap()
    wqkv_h = nc.dram_tensor("w_qkv", [DM, 3 * DM], F32, kind="ExternalInput").ap()
    wo_h = nc.dram_tensor("w_out", [DM, DM], F32, kind="ExternalInput").ap()
    wsyn_h = nc.dram_tensor("W_syn", [NF, DM], F32, kind="ExternalInput").ap()
    bsyn_h = nc.dram_tensor("b_syn", [NF], F32, kind="ExternalInput").ap()
    woutw_h = nc.dram_tensor("W_out_f", [DM, NF], F32, kind="ExternalInput").ap()
    bout_h = nc.dram_tensor("b_out", [DM], F32, kind="ExternalInput").ap()
    gamma_h = nc.dram_tensor("gamma", [DM], F32, kind="ExternalInput").ap()
    beta_h = nc.dram_tensor("beta", [DM], F32, kind="ExternalInput").ap()
    out_h = nc.dram_tensor("out", [S, DM], F32, kind="ExternalOutput").ap()
    x2_h = nc.dram_tensor("x2h", [S, DM], F32, kind="Internal").ap()
    qs_h = nc.dram_tensor("qs", [H, HD, S], BF16, kind="Internal").ap()
    ks_h = nc.dram_tensor("ks", [H, HD, S], BF16, kind="Internal").ap()
    vs_h = nc.dram_tensor("vs", [H, 128, TT, 128], BF16, kind="Internal").ap()
    wsynq_h = nc.dram_tensor("wsynq", [128, DC, NF], FP8, kind="Internal").ap()
    woutq_h = nc.dram_tensor("woutq", [NCH, 128, DM], FP8, kind="Internal").ap()
    sc_h = nc.dram_tensor("sc_h", [256], F32, kind="Internal").ap()
    dbgz_h = nc.dram_tensor("dbgz", [128, 2 * NF], F32, kind="ExternalOutput").ap()
    dbgscal_h = nc.dram_tensor("dbgscal", [128, 8], F32, kind="ExternalOutput").ap()
    dbgx2_h = nc.dram_tensor("dbgx2", [128, DM], F32, kind="ExternalOutput").ap()
    dbgq_h = nc.dram_tensor("dbgq", [128, 512], F32, kind="ExternalOutput").ap()
    dbgsr_h = nc.dram_tensor("dbgsr", [1, 300], F32, kind="ExternalOutput").ap()
    dbgst_h = nc.dram_tensor("dbgst", [128, 27], F32, kind="ExternalOutput").ap()
    dbgit_h = nc.dram_tensor("dbgit", [NIT, 128, 16], F32, kind="Internal").ap()
    dbghid_h = nc.dram_tensor("dbghid", [128, NF * 2], BF16, kind="Internal").ap()
    dbgh2_h = nc.dram_tensor("dbgh2", [128, DM], F32, kind="ExternalOutput").ap()
    dbgh2b_h = nc.dram_tensor("dbgh2b", [128, DM], F32, kind="Internal").ap()
    dbgln2_h = nc.dram_tensor("dbgln2", [2, 128, DM], BF16, kind="ExternalOutput").ap()
    dbgu_h = nc.dram_tensor("dbgu", [2, 128, DM], BF16, kind="ExternalOutput").ap()

    es = contextlib.ExitStack()
    sbt = lambda name, shape, dt: es.enter_context(nc.sbuf_tensor(name, shape, dt))

    consts = sbt("consts", [128, 8], F32)
    scal = sbt("scal", [128, 8], F32)
    ident = sbt("ident", [128, 128], BF16)
    maskbig = sbt("maskbig", [128, 896], BF16)
    jrow = sbt("jrow", [128, 16], F32)
    g_fm = sbt("g_fm", [128, 16], F32)
    b_fm = sbt("b_fm", [128, 16], F32)
    x2st = sbt("x2st", [128, 32], F32)
    stt = sbt("stt", [128, 32], F32)
    fin = sbt("fin", [128, 32], F32)
    small = sbt("small", [128, 16], F32)
    mint = sbt("mint", [128, 4], mybir.dt.int32)
    srow = sbt("srow", [1, 300], F32)
    accq = sbt("accq", [128, 128], F32)
    ones2 = sbt("ones2", [2, 128], BF16)
    bbsyn = sbt("bbsyn", [2, NF], BF16)
    bbout = sbt("bbout", [2, DM], BF16)
    dumpV = sbt("dumpV", [128, 4096], BF16)
    dumpS = sbt("dumpS", [128, 4096], BF16)
    AR1 = sbt("AR1", [128, 16384], F32)
    AR2 = sbt("AR2", [128, 8192], F32)
    AR3 = sbt("AR3", [128, 6144], F32)
    AR4 = sbt("AR4", [128, 6144], F32)
    ps = es.enter_context(nc.psum_tensor("ps", [128, 4096], F32))
    psT = ps[:, 3584:4096].bitcast(BF16)

    x_lnT = AR1[:].bitcast(BF16)
    zA = AR1
    hidT = AR1[:, 0:8192].bitcast(BF16)
    q_strip = AR2[:, 0:4096]
    q_acc = AR2[:, 4096:8192].bitcast(BF16)
    o_qcT = AR2[:, 0:4096].bitcast(BF16)
    y_sb = AR2[:, 4096:6144]
    a2x = AR2[:, 6144:8192]
    w_scr = AR2
    cdump = AR2[:].bitcast(BF16)
    bx2fin = AR2[:, 4096:6144]
    bstage = AR2[0:1, 0:8192]
    dumpS2 = AR2[:].bitcast(BF16)[:, 8192:16384]
    bout_sb = AR2[:, 6144:8192]

    a0x = AR3[:, 0:4096]
    a0ln = AR3[:, 4096:5120].bitcast(BF16)
    a1q = AR3[:, 0:1024].bitcast(BF16)
    a1k = AR3[:, 1024:2048].bitcast(BF16)
    a1vT = AR3[:, 2048:3072].bitcast(BF16)
    a1v = AR3[:, 3072:4224].bitcast(BF16)
    a2q = AR3[:, 0:256].bitcast(BF16)
    a2k = AR3[:, 256:1280].bitcast(BF16)
    a2v = AR3[:, 1280:2432].bitcast(BF16)
    a2at = AR3[:, 2432:3200].bitcast(BF16)
    a2ot = AR3[:, 3200:3264].bitcast(BF16)
    bx2ln = AR3[:, 0:2048].bitcast(BF16)
    bx2lnT = AR3[:, 2048:4096].bitcast(BF16)
    bh2 = AR3[:, 4096:6144]
    bln2T = AR3[:, 4096:5120].bitcast(BF16)
    bln2 = AR3[:, 5120:6144].bitcast(BF16)

    a1w = AR4[:, 0:3072].bitcast(BF16)
    a2wo = AR4[:, 0:2048].bitcast(BF16)
    a2yln = AR4[:, 2048:4096]
    bwsyn = AR4[:, 0:4096].bitcast(FP8)
    bwout = AR4[:, 4096:5120].bitcast(FP8)
    bu = AR4[:, 5120:6144].bitcast(BF16)

    sems = {}
    for sname in ["l", "lg", "v", "s", "t", "g", "lo", "lv", "lx", "lb"]:
        sems[sname] = es.enter_context(nc.semaphore(sname))

    ev = {}
    producer = {}

    ENGSEM = {"v": "v", "s": "s", "g": "g", "t": "t"}

    class Prog:
        def __init__(self, emit_eng, engines):
            self.emit = emit_eng
            self.engines = engines
            self.cnt = dict((k, 0) for k in sems)
            self.lastw = {}
            self.rec = []

        def op(self, eng, fn, waits=(), sig=None, n=1):
            chain = eng in ("v", "s", "g")
            esem = ENGSEM.get(eng)
            if chain and sig is None:
                sig = (esem, None)
                n = 1
            if self.emit == "REC":
                if chain:
                    self.rec.append((eng, tuple(waits) + ((esem, self.cnt[esem]),),
                                     sig, n, True))
                    self.cnt[sig[0]] += n
                else:
                    self.rec.append((eng, tuple(waits), sig, n, False))
                    if sig:
                        self.cnt[sig[0]] += n
                return
            if self.emit is None:
                if chain:
                    prev = self.cnt[esem]
                if sig:
                    sname, evn = sig
                    self.cnt[sname] += n
                    if evn:
                        ev[evn] = (sname, self.cnt[sname])
                        producer[evn] = "dma" if n >= 16 else eng
                return
            if self.emit != eng:
                if self.emit in ("v", "s", "g") or True:
                    pass
                if chain or sig:
                    sname = sig[0]
                    self.cnt[sname] += n
                return
            e = self.engines[eng]
            if chain:
                prev = self.cnt[esem]
                if prev > 0 and self.lastw.get(esem, -1) < prev:
                    e.wait_ge(sems[esem], prev)
                    self.lastw[esem] = prev
            for w in waits:
                sname, v = ev[w]
                if self.lastw.get(sname, -1) >= v:
                    continue
                e.wait_ge(sems[sname], v)
                self.lastw[sname] = v
            ins = fn(e)
            if sig:
                sname, evn = sig
                self.cnt[sname] += n
                if ins is not None:
                    ins.then_inc(sems[sname], n)

    V, SC, T, G, SY = "v", "s", "t", "g", "sy"

    def program(P):
        op = P.op

        def dma(q, out, in_, ev_name, waits=(), slow=False, qsem=None):
            sname = qsem or ("l" if q == SY else "lg")
            op(q, lambda e: e.dma_start(out=out, in_=in_,
                                        allow_slow_non_contiguous=slow),
               waits=waits, sig=(sname, ev_name), n=16)

        op(V, lambda e: e.memset(consts[:, 0:1], LNEPS))
        op(V, lambda e: e.memset(consts[:, 1:2], C30))
        op(V, lambda e: e.memset(consts[:, 2:3], SHIFT), sig=("v", "consts"))
        op(V, lambda e: e.memset(ones2[:], 1.0), sig=("v", "ones2"))
        op(G, lambda e: e.memset(ident[:], 0.0), sig=("g", "id0"))
        op(G, lambda e: e.affine_select(out=ident[:], in_=ident[:],
                                        compare_op=ALU.not_equal, fill=1.0, base=0,
                                        pattern=[[-1, 128]], channel_multiplier=1),
           waits=["id0"], sig=("g", "ident"))
        op(G, lambda e: e.memset(maskbig[:], 1.0), sig=("g", "mb0"))
        op(G, lambda e: e.affine_select(out=maskbig[:], in_=maskbig[:],
                                        compare_op=ALU.is_ge, fill=0.0, base=-384,
                                        pattern=[[1, 896]], channel_multiplier=-1),
           waits=["mb0"], sig=("g", "maskbig"))
        op(G, lambda e: e.iota(jrow[:], pattern=[[1, 16]], base=0, channel_multiplier=0,
                               allow_small_or_imprecise_dtypes=True), sig=("g", "jrow"))
        dma(SY, g_fm[:], gamma_h.rearrange("(dc p) -> p dc", p=128), "g_fm", slow=True)
        dma(SY, b_fm[:], beta_h.rearrange("(dc p) -> p dc", p=128), "b_fm", slow=True)

        for tt in range(TT):
            xa = a0x[:, (tt % 2) * 2048:(tt % 2) * 2048 + 2048]
            dma(SY, xa, x_h[tt * 128:(tt + 1) * 128, :], f"a0x_{tt}",
                waits=([f"a0ln_{tt - 2}"] if tt >= 2 else []) +
                      ([f"a0sum_{tt - 1}", f"a0sq_{tt - 1}"] if tt else []))
            op(V, lambda e, xa=xa: e.tensor_scalar(
                dumpV[:, 0:2048], xa, 1.0, None, op0=ALU.mult, op1=ALU.add,
                accum_out=small[:, 0:1]), waits=[f"a0x_{tt}"], sig=("v", f"a0sum_{tt}"))
            op(SC, lambda e, xa=xa: e.activation(
                dumpS[:, 0:2048], xa, AF.Square, accum_out=small[:, 1:2]),
                waits=[f"a0x_{tt}"], sig=("s", f"a0sq_{tt}"))
            op(V, lambda e: e.tensor_scalar(small[:, 2:3], small[:, 0:1], 1.0 / DM,
                                            None, op0=ALU.mult), waits=[f"a0sum_{tt}"])
            op(V, lambda e: e.scalar_tensor_tensor(
                small[:, 3:4], small[:, 2:3], -1.0, small[:, 2:3],
                op0=ALU.mult, op1=ALU.mult))
            op(V, lambda e: e.tensor_scalar(small[:, 4:5], small[:, 1:2], 1.0 / DM,
                                            None, op0=ALU.mult), waits=[f"a0sq_{tt}"])
            op(V, lambda e: e.tensor_tensor(small[:, 4:5], small[:, 4:5], small[:, 3:4],
                                            op=ALU.add), sig=("v", f"a0var_{tt}"))
            op(SC, lambda e: e.activation(small[:, 5:6], small[:, 4:5], AF.Sqrt,
                                          bias=consts[:, 0:1]),
               waits=[f"a0var_{tt}", "consts"], sig=("s", f"a0sd_{tt}"))
            op(V, lambda e: e.reciprocal(small[:, 6:7], small[:, 5:6]),
               waits=[f"a0sd_{tt}"])
            op(V, lambda e: e.scalar_tensor_tensor(
                small[:, 7:8], small[:, 2:3], -1.0, small[:, 6:7],
                op0=ALU.mult, op1=ALU.mult), sig=("v", f"a0rs_{tt}"))
            op(SC, lambda e, xa=xa: e.activation(a0ln[:], xa, AF.Identity,
                                                 bias=small[:, 7:8], scale=small[:, 6:7]),
               waits=[f"a0rs_{tt}"], sig=("s", f"a0ln_{tt}"))
            for gj in range(4):
                for j in range(4):
                    dcx = gj * 4 + j
                    tw = ["ident", f"a0ln_{tt}"]
                    if gj:
                        tw.append(f"a0tev_{tt}_{gj - 1}")
                    elif tt:
                        tw.append(f"a0tev_{tt - 1}_3")
                    op(T, lambda e, dcx=dcx, j=j: e.transpose(
                        psT[:, j * 128:(j + 1) * 128],
                        a0ln[:, dcx * 128:(dcx + 1) * 128], ident[:]),
                        waits=tw, sig=("t", f"a0tr_{tt}_{gj}_{j}") if j == 3 else None)
                op(V, lambda e, gj=gj, tt=tt: e.tensor_copy(
                    x_lnT.rearrange("p (dc t) -> p dc t", dc=DC)[:, gj * 4:(gj + 1) * 4,
                                                                tt * 128:(tt + 1) * 128],
                    psT[:, 0:512].rearrange("p (j c) -> p j c", j=4)),
                    waits=[f"a0tr_{tt}_{gj}_3"], sig=("v", f"a0tev_{tt}_{gj}"))
        op(V, lambda e: e.nop(),
           waits=[f"a0tev_{TT - 1}_{gj}" for gj in range(4)], sig=("v", "a0done"))

        if _KPH < 2:
            return
        NCK = 64
        for i in range(NCK):
            wb = q_strip[:, (i % 2) * 2048:(i % 2) * 2048 + 2048]
            dma(SY, wb, wsyn_h[i * 128:(i + 1) * 128, :], f"pre_s_{i}",
                waits=[f"pre_sa_{i - 2}"] if i >= 2 else ["a0done"])
            op(V, lambda e, wb=wb: e.tensor_scalar(
                dumpV[:, 0:2048], wb, 0.0, None,
                op0=ALU.max, op1=ALU.add, accum_out=small[:, 14:15]),
                waits=[f"pre_s_{i}"], sig=("v", f"pre_sp_{i}"))
            op(V, lambda e, wb=wb: e.tensor_scalar(
                dumpV[:, 0:2048], wb, 0.0, None,
                op0=ALU.min, op1=ALU.add, accum_out=small[:, 15:16]),
                sig=("v", f"pre_sn_{i}"))
            op(V, lambda e, i=i: e.tensor_tensor(
                accq[:, i:i + 1], small[:, 14:15], small[:, 15:16],
                op=ALU.subtract), waits=[f"pre_sp_{i}", f"pre_sn_{i}"],
                sig=("v", f"pre_sa_{i}"))
        for i in range(NCK):
            wb = AR4[:, (i % 2) * 2048:(i % 2) * 2048 + 2048]
            dma(SY, wb, woutw_h.rearrange("a (b c) -> (a b) c", b=4)
                [i * 128:(i + 1) * 128, :],
                f"pre_o_{i}", waits=[f"pre_oa_{i - 2}"] if i >= 2 else ["a0done"])
            op(SC, lambda e, wb=wb, i=i: e.activation(
                dumpS[:, 0:2048], wb, AF.Abs, accum_out=accq[:, 64 + i:65 + i]),
                waits=[f"pre_o_{i}"], sig=("s", f"pre_oa_{i}"))
        op(V, lambda e: e.reduce_sum(small[:, 0:1], accq[:, 0:64], axis=AXX),
           waits=[f"pre_sa_{NCK - 1}"], sig=("v", "pre_sred"))
        op(V, lambda e: e.reduce_sum(small[:, 1:2], accq[:, 64:128], axis=AXX),
           waits=[f"pre_oa_{NCK - 1}"], sig=("v", "pre_ored"))
        dma(SY, sc_h[0:128].rearrange("(p o) -> p o", p=128), small[:, 0:1],
            "pre_c2hA", waits=["pre_sred", "pre_ored"])
        dma(SY, sc_h[128:256].rearrange("(p o) -> p o", p=128), small[:, 1:2],
            "pre_c2hB", waits=["pre_c2hA"])
        dma(SY, srow[:, 0:256], sc_h[0:256].rearrange("(o n) -> o n", o=1),
            "pre_h2r", waits=["pre_c2hB"])
        op(V, lambda e: e.reduce_sum(srow[:, 256:257], srow[:, 0:128], axis=AXX),
           waits=["pre_h2r"])
        op(V, lambda e: e.reduce_sum(srow[:, 257:258], srow[:, 128:256], axis=AXX))
        op(V, lambda e: e.tensor_scalar(srow[:, 259:261], srow[:, 256:258],
                                        1.0 / (NF * DM), None, op0=ALU.mult))
        op(V, lambda e: e.tensor_scalar(srow[:, 261:263], srow[:, 259:261],
                                        1e-8, None, op0=ALU.add))
        op(V, lambda e: e.reciprocal(srow[:, 263:265], srow[:, 261:263]))
        op(V, lambda e: e.reciprocal(srow[:, 268:270], srow[:, 259:261]),
           sig=("v", "pre_pack"))
        dma(SY, sc_h[0:2].rearrange("(o n) -> o n", o=1), srow[:, 263:265],
            "pre_s2h1", waits=["pre_pack"])
        dma(SY, sc_h[2:4].rearrange("(o n) -> o n", o=1), srow[:, 268:270],
            "pre_s2h2", waits=["pre_pack"])
        dma(SY, sc_h[4:6].rearrange("(o n) -> o n", o=1), srow[:, 259:261],
            "pre_s2h", waits=["pre_pack"])
        dma(SY, scal[:, 0:6],
            sc_h[0:6].rearrange("(o n) -> o n", o=1).broadcast_to([128, 6]),
            "pre_bc", waits=["pre_s2h1", "pre_s2h2", "pre_s2h"], slow=True)
        if _DBG:
            dma(SY, dbgscal_h, scal[:, 0:8], "dbgscal", waits=["pre_bc"])
            dma(SY, dbgsr_h, srow[:, 0:300], "dbgsr", waits=["pre_pack"])
        op(V, lambda e: e.nop(), waits=["pre_bc"], sig=("v", "scal_syn"))
        op(V, lambda e: e.nop(), waits=["pre_bc"], sig=("v", "scal_out"))

        if _KPH < 3:
            return
        def quant_pass(qb, inv_ap, ev_pfx, waits_l):
            op(V, lambda e: e.tensor_scalar(
                dumpV[:, 0:2048], qb, inv_ap, 0.5, op0=ALU.mult, op1=ALU.is_gt),
                waits=waits_l, sig=("v", ev_pfx + "a"))
            op(V, lambda e: e.scalar_tensor_tensor(
                qb, qb, inv_ap, qb, op0=ALU.mult, op1=ALU.bypass))
            op(V, lambda e: e.tensor_scalar(
                dumpV[:, 2048:4096], qb, -1.0, 0.5, op0=ALU.mult, op1=ALU.is_gt))
            op(V, lambda e: e.tensor_tensor(
                qb.bitcast(BF16)[:, 0:2048], dumpV[:, 0:2048], dumpV[:, 2048:4096],
                op=ALU.subtract), sig=("v", ev_pfx + "q"))

        for dc in range(DC):
            for q4 in range(4):
                idx = dc * 4 + q4
                qb = q_strip[:, (idx % 2) * 2048:(idx % 2) * 2048 + 2048]
                lw = ([f"pre_sa_{NCK - 1}"] if idx == 0 else
                      [f"qs_ev_{dc}_{q4 - 1}_3"] if q4 else
                      [f"qs_ev_{dc - 1}_3_3"])
                dma(SY, qb.rearrange("p (nt d) -> p nt d", nt=16),
                    wsyn_h[q4 * 2048:(q4 + 1) * 2048, dc * 128:(dc + 1) * 128]
                    .rearrange("(nt p) d -> p nt d", p=128), f"qs_l_{dc}_{q4}",
                    waits=lw)
                quant_pass(qb, scal[:, 0:1], f"qs_{dc}_{q4}",
                           [f"qs_l_{dc}_{q4}", "scal_syn"])
                for gj in range(4):
                    for j in range(4):
                        nt = gj * 4 + j
                        tw = ["ident", f"qs_{dc}_{q4}q"]
                        if gj:
                            tw.append(f"qs_ev_{dc}_{q4}_{gj - 1}")
                        elif q4:
                            tw.append(f"qs_ev_{dc}_{q4 - 1}_3")
                        elif dc:
                            tw.append(f"qs_ev_{dc - 1}_3_3")
                        else:
                            tw.append(f"a0tev_{TT - 1}_3")
                        op(T, lambda e, qb=qb, nt=nt, j=j: e.transpose(
                            psT[:, j * 128:(j + 1) * 128],
                            qb.bitcast(BF16)[:, nt * 128:(nt + 1) * 128], ident[:]),
                            waits=tw, sig=("t", f"qs_tr_{dc}_{q4}_{gj}") if j == 3 else None)
                    vw = [f"qs_tr_{dc}_{q4}_{gj}"]
                    if gj == 0 and q4 == 0 and dc:
                        vw.append(f"qs_w_{dc - 1}")
                    op(V, lambda e, q4=q4, gj=gj: e.tensor_copy(
                        q_acc[:, q4 * 2048 + gj * 512:q4 * 2048 + (gj + 1) * 512],
                        psT[:, 0:512]),
                        waits=vw, sig=("v", f"qs_ev_{dc}_{q4}_{gj}"))
            dma(G, wsynq_h[:, dc, :], q_acc[:, 0:NF], f"qs_w_{dc}",
                waits=[f"qs_ev_{dc}_3_3"])
        for ncx in range(NCH):
            qb = q_strip[:, (ncx % 2) * 2048:(ncx % 2) * 2048 + 2048]
            lw = [f"qo_ev_{ncx - 1}_3"] if ncx else [f"qs_ev_{DC - 1}_3_3",
                                                      f"pre_oa_{NCK - 1}"]
            dma(SY, qb.rearrange("p (dt n) -> p dt n", dt=16),
                woutw_h[:, ncx * 128:(ncx + 1) * 128]
                .rearrange("(dt p) n -> p dt n", p=128), f"qo_l_{ncx}", waits=lw)
            quant_pass(qb, scal[:, 1:2], f"qo_{ncx}",
                       [f"qo_l_{ncx}", "scal_out"])
            for gj in range(4):
                for j in range(4):
                    dt_ = gj * 4 + j
                    tw = [f"qo_{ncx}q"]
                    if gj:
                        tw.append(f"qo_ev_{ncx}_{gj - 1}")
                    elif ncx:
                        tw.append(f"qo_ev_{ncx - 1}_3")
                    else:
                        tw.append(f"qs_ev_{DC - 1}_3_3")
                    op(T, lambda e, qb=qb, dt_=dt_, j=j: e.transpose(
                        psT[:, j * 128:(j + 1) * 128],
                        qb.bitcast(BF16)[:, dt_ * 128:(dt_ + 1) * 128], ident[:]),
                        waits=tw, sig=("t", f"qo_tr_{ncx}_{gj}") if j == 3 else None)
                vw = [f"qo_tr_{ncx}_{gj}"]
                if gj == 0 and ncx:
                    vw.append(f"qo_w_{ncx - 1}")
                op(V, lambda e, gj=gj: e.tensor_copy(
                    q_acc[:, gj * 512:(gj + 1) * 512], psT[:, 0:512]),
                    waits=vw, sig=("v", f"qo_ev_{ncx}_{gj}"))
            dma(G, woutq_h[ncx], q_acc[:, 0:DM], f"qo_w_{ncx}",
                waits=[f"qo_ev_{ncx}_3"])

        if _KPH < 4:
            return
        dma(SY, bstage[:, 0:NF], bsyn_h.rearrange("(o n) -> o n", o=1), "bsyn_l",
            waits=[f"qo_w_{NCH - 1}", f"qo_ev_{NCH - 1}_3"])
        op(V, lambda e: e.tensor_scalar(bstage[:, 0:NF], bstage[:, 0:NF],
                                        scal[0:1, 2:3], None, op0=ALU.mult),
           waits=["bsyn_l", "scal_syn"])
        op(V, lambda e: e.tensor_copy(bbsyn[0:1, :], bstage[:, 0:NF]), sig=("v", "bbh"))
        op(V, lambda e: e.tensor_tensor(bstage[:, 0:NF], bstage[:, 0:NF], bbsyn[0:1, :],
                                        op=ALU.subtract), waits=["bbh"])
        op(V, lambda e: e.tensor_copy(dumpS[0:1, 0:4096], bstage[:, 0:4096]),
           sig=("v", "bblo0"))
        op(V, lambda e: e.tensor_copy(dumpS[0:1, 4096:8192]
                                      if False else dumpS[0:1, 0:4096][:, 0:4096],
                                      bstage[:, 4096:8192]), waits=["bblo0x"],
           sig=("v", "bblo1"))
        dma(G, bbsyn[1:2, 0:4096], dumpS[0:1, 0:4096], "bblo0x", waits=["bblo0"])
        dma(G, bbsyn[1:2, 4096:8192], dumpS[0:1, 0:4096], "bbsyn", waits=["bblo1"])
        dma(SY, bstage[:, 0:DM], bout_h.rearrange("(o n) -> o n", o=1), "bout_l",
            waits=["bbsyn"])
        op(V, lambda e: e.tensor_scalar(bstage[:, 0:DM], bstage[:, 0:DM],
                                        scal[0:1, 3:4], None, op0=ALU.mult),
           waits=["bout_l", "scal_out"])
        op(V, lambda e: e.tensor_copy(bbout[0:1, :], bstage[:, 0:DM]), sig=("v", "boh"))
        op(V, lambda e: e.tensor_tensor(bstage[:, 0:DM], bstage[:, 0:DM], bbout[0:1, :],
                                        op=ALU.subtract), waits=["boh"])
        op(V, lambda e: e.tensor_copy(dumpS[0:1, 0:2048], bstage[:, 0:DM]),
           sig=("v", "bolo"))
        dma(G, bbout[1:2, 0:2048], dumpS[0:1, 0:2048], "bbout", waits=["bolo"])


        if _KPH < 5:
            return
        for kt in range(TT):
            op(V, lambda e, kt=kt: e.memset(a1v[:, kt * 144 + 128:kt * 144 + 129], 1.0),
               waits=["a0done", f"a0ln_{TT - 1}"] if kt == 0 else [])
        op(V, lambda e: e.nop(), sig=("v", "vones1"))
        for h in range(H):
            for i3 in range(3):
                wv = a1w[:, i3 * 2048:(i3 + 1) * 2048]
                col0 = i3 * DM + h * 128
                dma(G, wv.rearrange("p (dc c) -> p dc c", dc=DC),
                    wqkv_h[:, col0:col0 + 128]
                    .rearrange("(dc p) c -> p dc c", p=128), f"a1w_{h}_{i3}",
                    waits=[f"a1mmall_{h - 1}_{i3}"] if h else [f"qo_w_{NCH - 1}"])
            for i3 in range(3):
                wv = a1w[:, i3 * 2048:(i3 + 1) * 2048]
                dst = (a1q, a1k, a1vT)[i3]
                for tc in range(4):
                    bank = i3 * 1024 + (tc % 2) * 512
                    for dcx in range(DC):
                        tw = []
                        if dcx == 0:
                            tw = ["a0done", f"a1w_{h}_{i3}", f"a1w_{h}_2"]
                            if tc >= 2:
                                tw.append(f"a1ev_{h}_{i3}_{tc - 2}")
                            elif h:
                                tw.append(f"a1ev_{h - 1}_{i3}_{tc + 2}")
                        op(T, lambda e, wv=wv, dcx=dcx, tc=tc, bank=bank: e.matmul(
                            ps[:, bank:bank + 512],
                            wv.rearrange("p (dc c) -> p dc c", dc=DC)[:, dcx, :],
                            x_lnT.rearrange("p (dc t) -> p dc t", dc=DC)
                            [:, dcx, tc * 512:(tc + 1) * 512],
                            start=(dcx == 0), stop=(dcx == DC - 1)),
                            waits=tw,
                            sig=("t", f"a1mm_{h}_{i3}_{tc}") if dcx == DC - 1 else None)
                    evw = [f"a1mm_{h}_{i3}_{tc}"]
                    if h and tc == 0 and i3 == 0:
                        evw.append(f"a1qs_{h - 1}")
                    if h and tc == 0 and i3 == 1:
                        evw.append(f"a1ks_{h - 1}")
                    op(V, lambda e, dst=dst, tc=tc, bank=bank: e.tensor_copy(
                        dst[:, tc * 512:(tc + 1) * 512], ps[:, bank:bank + 512]),
                        waits=evw, sig=("v", f"a1ev_{h}_{i3}_{tc}"))
                op(V, lambda e: e.nop(),
                   waits=[f"a1ev_{h}_{i3}_{tc}" for tc in range(4)],
                   sig=("v", f"a1mmall_{h}_{i3}"))
            dma(SY, qs_h[h], a1q[:, 0:2048], f"a1qs_{h}", waits=[f"a1mmall_{h}_0"],
                qsem="lv")
            dma(SY, ks_h[h], a1k[:, 0:2048], f"a1ks_{h}", waits=[f"a1mmall_{h}_1"],
                qsem="lv")
            for gj in range(4):
                for j in range(4):
                    kt = gj * 4 + j
                    tw = ["ident", f"a1mmall_{h}_2"]
                    if gj:
                        tw.append(f"a1vev_{h}_{gj - 1}")
                    elif h:
                        tw.append(f"a1vev_{h - 1}_3")
                    else:
                        tw.append(f"qo_ev_{NCH - 1}_3")
                    op(T, lambda e, kt=kt, j=j: e.transpose(
                        psT[:, j * 128:(j + 1) * 128],
                        a1vT[:, kt * 128:(kt + 1) * 128], ident[:]),
                        waits=tw, sig=("t", f"a1vtr_{h}_{gj}") if j == 3 else None)
                vw = [f"a1vtr_{h}_{gj}", "vones1"]
                if gj == 0 and h:
                    vw.append(f"a1vs_{h - 1}")
                op(V, lambda e, gj=gj: e.tensor_copy(
                    a1v.rearrange("p (kt c) -> p kt c", kt=TT)
                    [:, gj * 4:(gj + 1) * 4, 0:128],
                    psT[:, 0:512].rearrange("p (j c) -> p j c", j=4)),
                    waits=vw, sig=("v", f"a1vev_{h}_{gj}"))
            dma(SY, vs_h[h], a1v.rearrange("p (kt c) -> p kt c", kt=TT)[:, :, 0:128],
                f"a1vs_{h}", waits=[f"a1vev_{h}_3"], qsem="lv")
        op(V, lambda e: e.nop(), waits=[f"a1vev_{H - 1}_3"], sig=("v", "a1done"))

        if _KPH < 6:
            return
        for kt in range(TT):
            op(V, lambda e, kt=kt: e.memset(a2v[:, kt * 144 + 128:kt * 144 + 129], 1.0),
               waits=["a1done"] if kt == 0 else [])
        op(V, lambda e: e.nop(), sig=("v", "vones2"))
        for qc in range(_KQC):
            nkt = 4 * qc + 4
            for h in range(_KH):
                pv = f"{qc}_{h}"
                prev = (f"a2fr_{qc}_{h - 1}" if h >= 1 else
                        (f"a2fr_{qc - 1}_{H - 1}" if qc else None))
                lw = ([prev] if prev else []) + ([f"a1vs_{H - 1}"] if qc == 0 else [])
                dma(SY, a2q[:, 0:512], qs_h[h][:, qc * 512:(qc + 1) * 512],
                    f"a2q_{pv}", waits=lw)
                lw = ([prev] if prev else []) + ([f"a1vs_{H - 1}"] if qc == 0 else [])
                dma(SY, a2k[:, 0:nkt * 128], ks_h[h][:, 0:nkt * 128], f"a2k_{pv}",
                    waits=lw)
                lw = (([prev] if prev else []) +
                      ([f"a1vs_{H - 1}"] if qc == 0 else []) + ["vones2"])
                dma(SY, a2v.rearrange("p (kt c) -> p kt c", kt=TT)[:, 0:nkt, 0:128],
                    vs_h[h][:, 0:nkt, :], f"a2v_{pv}", waits=lw)
                for kt in range(nkt):
                    sb_ = (kt % 2) * 512
                    at = a2at[:, (kt % 3) * 512:(kt % 3) * 512 + 512]
                    tw = [f"a2q_{pv}", f"a2k_{pv}", f"a2v_{pv}"]
                    if kt >= 2:
                        tw.append(f"a2ex_{pv}_{kt - 2}")
                    elif h:
                        tw.append(f"a2hd_{qc}_{h - 1}")
                    elif qc:
                        tw.append(f"a2hd_{qc - 1}_{H - 1}")
                        if kt == 0:
                            tw.append(f"a2pV_{qc - 1}")
                            tw.append(f"a2pS_{qc - 1}")
                    else:
                        tw.append("a1done")
                    op(T, lambda e, kt=kt, sb_=sb_: e.matmul(
                        ps[:, sb_:sb_ + 512], a2k[:, kt * 128:(kt + 1) * 128],
                        a2q[:, 0:512], start=True, stop=True),
                        waits=tw, sig=("t", f"a2sc_{pv}_{kt}"))
                    sw = [f"a2sc_{pv}_{kt}", "consts"]
                    if kt >= 3:
                        sw.append(f"a2atf_{pv}_{kt - 3}")
                    op(SC, lambda e, at=at, sb_=sb_: e.activation(
                        at, ps[:, sb_:sb_ + 512], AF.Exp, bias=consts[:, 1:2],
                        scale=ESC), waits=sw, sig=("s", f"a2ex_{pv}_{kt}"))
                    if kt >= 4 * qc:
                        r = kt - 4 * qc
                        op(V, lambda e, at=at, r=r: e.tensor_tensor(
                            at, at, maskbig[:, 384 - 128 * r:896 - 128 * r],
                            op=ALU.mult),
                            waits=[f"a2ex_{pv}_{kt}", "maskbig"],
                            sig=("v", f"a2mk_{pv}_{kt}"))
                        evx = f"a2mk_{pv}_{kt}"
                    else:
                        evx = f"a2ex_{pv}_{kt}"
                    for qt in range(4):
                        if kt > 4 * qc + qt:
                            continue
                        ob = 1024 + qt * 512
                        tw = [evx, f"a2v_{pv}"]
                        if kt == 0 and h:
                            tw.append(f"a2e1_{qc}_{h - 1}_3")
                        elif kt == 0 and qc:
                            tw.append(f"a2e1_{qc - 1}_{H - 1}_3")
                        sig = None
                        if kt == 4 * qc + qt:
                            sig = ("t", f"a2avd_{pv}_{qt}")
                        op(T, lambda e, at=at, qt=qt, ob=ob, kt=kt: e.matmul(
                            ps[:, ob:ob + 129], at[:, qt * 128:(qt + 1) * 128],
                            a2v[:, kt * 144:kt * 144 + 129],
                            start=(kt == 0), stop=(kt == 4 * qc + qt)),
                            waits=tw, sig=sig)
                    op(T, lambda e: e.nop(), sig=("t", f"a2atf_{pv}_{kt}"))
                for qt in range(4):
                    ob = 1024 + qt * 512
                    rw = [f"a2avd_{pv}_3"]
                    if qt:
                        rw.append(f"a2e1_{pv}_{qt - 1}")
                    op(V, lambda e, ob=ob: e.reciprocal(
                        small[:, 10:11], ps[:, ob + 128:ob + 129]),
                        waits=rw, sig=("v", f"a2r_{pv}_{qt}"))
                    sw = [f"a2r_{pv}_{qt}"]
                    if qt:
                        sw.append(f"a2o_{pv}_{qt - 1}")
                    op(SC, lambda e, ob=ob: e.activation(
                        a2ot[:, 0:128], ps[:, ob:ob + 128], AF.Identity,
                        scale=small[:, 10:11]),
                        waits=sw, sig=("s", f"a2e1_{pv}_{qt}"))
                    tw = ["ident", f"a2e1_{pv}_{qt}"]
                    if qt:
                        tw.append(f"a2o_{pv}_{qt - 1}")
                    op(T, lambda e: e.transpose(
                        ps[:, 3072:3584].bitcast(BF16)[:, 0:128], a2ot[:, 0:128],
                        ident[:]), waits=tw, sig=("t", f"a2tr_{pv}_{qt}"))
                    ow = [f"a2tr_{pv}_{qt}"]
                    if qc == 0 and h == 0 and qt == 0:
                        ow.append("bbout")
                    op(V, lambda e, qt=qt, h=h: e.tensor_copy(
                        o_qcT[:, h * 512 + qt * 128:h * 512 + (qt + 1) * 128],
                        ps[:, 3072:3584].bitcast(BF16)[:, 0:128]),
                        waits=ow, sig=("v", f"a2o_{pv}_{qt}"))
                op(V, lambda e: e.nop(),
                   waits=[f"a2o_{pv}_{qt}" for qt in range(4)],
                   sig=("v", f"a2done_{qc}_{h}"))
                op(SC, lambda e: e.nop(), waits=[f"a2done_{qc}_{h}"],
                   sig=("s", f"a2hd_{qc}_{h}"))
                op(V, lambda e: e.nop(), waits=[f"a2hd_{qc}_{h}"],
                   sig=("v", f"a2fr_{qc}_{h}"))
            if qc == 0:
                if _DBG:
                    dma(SY, dbgq_h, o_qcT[:, 0:1024].bitcast(F32) if False else AR2[:, 0:512],
                        "dbgq", waits=[f"a2done_{qc}_{_KH - 1}"])
            if not _KWOUT:
                continue
            for sp in range(2):
                for dcx in range(DC):
                    wob = a2wo[:, (dcx % 2) * 2048:(dcx % 2) * 2048 + 2048]
                    lw = []
                    if dcx >= 2:
                        lw = [f"a2wm_{qc}_{sp}_{dcx - 2}"]
                    elif sp:
                        lw = [f"a2wm_{qc}_0_{DC - 2 + dcx}"]
                    elif qc:
                        lw = [f"a2wm_{qc - 1}_1_{DC - 2 + dcx}"]
                    else:
                        lw = ["a1done"]
                    dma(G, wob, wo_h[dcx * 128:(dcx + 1) * 128, :],
                        f"a2wo_{qc}_{sp}_{dcx}", waits=lw)
                    for qi in range(2):
                        qt = sp * 2 + qi
                        for d2c in range(4):
                            bank = qi * 2048 + d2c * 512
                            tw = []
                            if qi == 0 and d2c == 0:
                                tw = [f"a2wo_{qc}_{sp}_{dcx}"]
                                if dcx == 0:
                                    tw.append(f"a2done_{qc}_{_KH - 1}")
                            if dcx == 0 and _KWL >= 2:
                                if sp:
                                    tw.append(f"a2yev_{qc}_0_{qi}_{d2c}")
                                    tw.append(f"a2ysq_{qc}_0_{qi}_{d2c}")
                                elif qc:
                                    tw.append(f"a2yev_{qc - 1}_1_{qi}_{d2c}")
                                    tw.append(f"a2ysq_{qc - 1}_1_{qi}_{d2c}")
                            op(T, lambda e, wob=wob, qt=qt, dcx=dcx, bank=bank,
                               d2c=d2c: e.matmul(
                                ps[:, bank:bank + 512],
                                o_qcT[:, dcx * 512 + qt * 128:dcx * 512 + (qt + 1) * 128],
                                wob[:, d2c * 512:(d2c + 1) * 512],
                                start=(dcx == 0), stop=(dcx == DC - 1)),
                                waits=tw,
                                sig=("t", f"a2wm_{qc}_{sp}_{dcx}")
                                if qi == 1 and d2c == 3 else None)
                for qi in range(2):
                    if _KWL < 2:
                        continue
                    qt = sp * 2 + qi
                    qtt = qc * 4 + qt
                    for d2c in range(4):
                        bank = qi * 2048 + d2c * 512
                        vw = [f"a2wm_{qc}_{sp}_{DC - 1}"]
                        if _KWL >= 3:
                            if qi == 0 and d2c == 0 and qtt >= 1:
                                vw += [f"a2x2w_{qtt - 1}", f"a2x2sq_{qtt - 1}"]
                            if qi == 1 and d2c == 0:
                                vw.append(f"a2x2w_{qtt - 1}")
                                vw.append(f"a2x2sq_{qtt - 1}")
                        op(V, lambda e, bank=bank, d2c=d2c: e.tensor_scalar(
                            y_sb[:, d2c * 512:(d2c + 1) * 512], ps[:, bank:bank + 512],
                            0.0, None, op0=ALU.add, op1=ALU.add,
                            accum_out=small[:, d2c:d2c + 1]),
                            waits=vw, sig=("v", f"a2yev_{qc}_{sp}_{qi}_{d2c}"))
                        sqw = [f"a2wm_{qc}_{sp}_{DC - 1}", f"a2yev_{qc}_{sp}_{qi}_{d2c}"]
                        if qi == 1 and d2c == 0 and _KWL >= 3:
                            sqw.append(f"a2yred_{qc}_{sp}_0")
                        op(SC, lambda e, bank=bank, d2c=d2c: e.activation(
                            dumpS[:, 0:512], y_sb[:, d2c * 512:(d2c + 1) * 512],
                            AF.Square, accum_out=small[:, 4 + d2c:5 + d2c]),
                            waits=sqw,
                            sig=("s", f"a2ysq_{qc}_{sp}_{qi}_{d2c}"))
                    if _KWL < 3:
                        continue
                    op(V, lambda e: e.reduce_sum(small[:, 8:9], small[:, 0:4], axis=AXX),
                       waits=[f"a2yev_{qc}_{sp}_{qi}_{d2c}" for d2c in range(4)])
                    op(V, lambda e: e.reduce_sum(small[:, 9:10], small[:, 4:8], axis=AXX),
                       waits=[f"a2ysq_{qc}_{sp}_{qi}_{d2c}" for d2c in range(4)],
                       sig=("v", f"a2yred_{qc}_{sp}_{qi}"))
                    op(V, lambda e: e.tensor_scalar(small[:, 8:9], small[:, 8:9],
                                                    1.0 / DM, None, op0=ALU.mult))
                    op(V, lambda e: e.tensor_scalar(small[:, 9:10], small[:, 9:10],
                                                    1.0 / DM, None, op0=ALU.mult))
                    op(V, lambda e: e.scalar_tensor_tensor(
                        small[:, 10:11], small[:, 8:9], -1.0, small[:, 8:9],
                        op0=ALU.mult, op1=ALU.mult))
                    op(V, lambda e: e.tensor_tensor(small[:, 9:10], small[:, 9:10],
                                                    small[:, 10:11], op=ALU.add),
                       sig=("v", f"a2var_{qtt}"))
                    op(SC, lambda e: e.activation(small[:, 11:12], small[:, 9:10],
                                                  AF.Sqrt, bias=consts[:, 0:1]),
                       waits=[f"a2var_{qtt}"], sig=("s", f"a2sd_{qtt}"))
                    op(V, lambda e: e.reciprocal(small[:, 12:13], small[:, 11:12]),
                       waits=[f"a2sd_{qtt}"])
                    op(V, lambda e: e.scalar_tensor_tensor(
                        small[:, 13:14], small[:, 8:9], -1.0, small[:, 12:13],
                        op0=ALU.mult, op1=ALU.mult), sig=("v", f"a2rs_{qtt}"))
                    op(SC, lambda e: e.activation(a2yln[:, 0:2048], y_sb[:, 0:2048],
                                                  AF.Identity, bias=small[:, 13:14],
                                                  scale=small[:, 12:13]),
                       waits=[f"a2rs_{qtt}"], sig=("s", f"a2yln_{qtt}"))
                    dma(SY, a2x[:, 0:2048], x_h[qtt * 128:(qtt + 1) * 128, :],
                        f"a2x_{qtt}", waits=[f"a2x2_{qtt - 1}"] if qtt else [],
                        qsem="lv")
                    op(V, lambda e: e.scalar_tensor_tensor(
                        y_sb[:, 0:2048], a2yln[:, 0:2048], 1.0, a2x[:, 0:2048],
                        op0=ALU.mult, op1=ALU.add, accum_out=small[:, 14:15]),
                        waits=[f"a2yln_{qtt}", f"a2x_{qtt}"], sig=("v", f"a2x2_{qtt}"))
                    op(SC, lambda e: e.activation(dumpS[:, 0:2048], y_sb[:, 0:2048],
                                                  AF.Square, accum_out=small[:, 15:16]),
                       waits=[f"a2x2_{qtt}"], sig=("s", f"a2x2sq_{qtt}"))
                    if qtt == 0 and _DBG:
                        dma(SY, dbgx2_h, y_sb[:, 0:2048], "dbgx2",
                            waits=[f"a2x2_{qtt}"])
                    dma(SY, x2_h[qtt * 128:(qtt + 1) * 128, :], y_sb[:, 0:2048],
                        f"a2x2w_{qtt}", waits=[f"a2x2_{qtt}"], qsem="lx")
                    op(V, lambda e: e.tensor_scalar(small[:, 8:9], small[:, 14:15],
                                                    1.0 / DM, None, op0=ALU.mult),
                       waits=[f"a2x2_{qtt}"])
                    op(V, lambda e: e.tensor_scalar(small[:, 9:10], small[:, 15:16],
                                                    1.0 / DM, None, op0=ALU.mult),
                       waits=[f"a2x2sq_{qtt}"])
                    op(V, lambda e: e.scalar_tensor_tensor(
                        small[:, 10:11], small[:, 8:9], -1.0, small[:, 8:9],
                        op0=ALU.mult, op1=ALU.mult))
                    op(V, lambda e: e.tensor_tensor(small[:, 9:10], small[:, 9:10],
                                                    small[:, 10:11], op=ALU.add),
                       sig=("v", f"bvar_{qtt}"))
                    op(SC, lambda e: e.activation(small[:, 11:12], small[:, 9:10],
                                                  AF.Sqrt, bias=consts[:, 0:1]),
                       waits=[f"bvar_{qtt}"], sig=("s", f"bsd_{qtt}"))
                    op(V, lambda e, qtt=qtt: e.reciprocal(
                        x2st[:, qtt * 2:qtt * 2 + 1], small[:, 11:12]),
                        waits=[f"bsd_{qtt}"])
                    op(V, lambda e, qtt=qtt: e.scalar_tensor_tensor(
                        x2st[:, qtt * 2 + 1:qtt * 2 + 2], small[:, 8:9], -1.0,
                        x2st[:, qtt * 2:qtt * 2 + 1], op0=ALU.mult, op1=ALU.mult),
                        sig=("v", f"bst_{qtt}"))
            op(V, lambda e: e.nop(),
               waits=([f"a2yev_{qc}_{sp2}_{qi2}_{d2c2}" for sp2 in range(2)
                      for qi2 in range(2) for d2c2 in range(4)] if _KWL >= 2 else
                      [f"a2wm_{qc}_{sp2}_{DC - 1}" for sp2 in range(2)]),
               sig=("v", f"a2pV_{qc}"))
            op(SC, lambda e: e.nop(),
               waits=([f"a2ysq_{qc}_{sp2}_{qi2}_{d2c2}" for sp2 in range(2)
                      for qi2 in range(2) for d2c2 in range(4)] if _KWL >= 2 else []),
               sig=("s", f"a2pS_{qc}"))
        op(V, lambda e: e.nop(),
           waits=([f"bst_{q}" for q in range(TT)] if (_KQC == 4 and _KWOUT) else []) +
                 [f"qs_w_{DC - 1}", f"qo_w_{NCH - 1}", "bbsyn", "bbout", "jrow"],
           sig=("v", "a_done"))


        if _KPH < 7:
            return
        for g in range(NGRP):
            gt0 = g * 2
            for i in range(2):
                qtt = gt0 + i
                lw = (["a_done", f"a2x2w_{TT - 1}"] if g == 0 and i == 0
                      else [f"b1ln_{g}_{i - 1}"] if i else [f"bfin_{g - 1}_1"])
                dma(SY, bx2fin, x2_h[qtt * 128:(qtt + 1) * 128, :], f"b1x_{g}_{i}",
                    waits=lw, qsem="lb")
                op(SC, lambda e, i=i, qtt=qtt: e.activation(
                    bx2ln[:, i * 2048:(i + 1) * 2048], bx2fin, AF.Identity,
                    bias=x2st[:, qtt * 2 + 1:qtt * 2 + 2],
                    scale=x2st[:, qtt * 2:qtt * 2 + 1]),
                    waits=[f"b1x_{g}_{i}", f"bst_{qtt}"], sig=("s", f"b1ln_{g}_{i}"))
                for gj in range(4):
                    for j in range(4):
                        dcx = gj * 4 + j
                        tw = ["ident", f"b1ln_{g}_{i}"]
                        if gj:
                            tw.append(f"b1ev_{g}_{i}_{gj - 1}")
                        elif i:
                            tw.append(f"b1ev_{g}_0_3")
                        elif g:
                            tw.append(f"b5bb_{g - 1}_1_3")
                        else:
                            tw.append("a2yev_3_1_1_3")
                            tw.append("a2ysq_3_1_1_3")
                        op(T, lambda e, dcx=dcx, j=j, i=i: e.transpose(
                            psT[:, j * 128:(j + 1) * 128],
                            bx2ln[:, i * 2048 + dcx * 128:i * 2048 + (dcx + 1) * 128],
                            ident[:]),
                            waits=tw, sig=("t", f"b1tr_{g}_{i}_{gj}") if j == 3 else None)
                    op(V, lambda e, gj=gj, i=i: e.tensor_copy(
                        bx2lnT.rearrange("p (dc t) -> p dc t", dc=DC)
                        [:, gj * 4:(gj + 1) * 4, i * 128:(i + 1) * 128],
                        psT[:, 0:512].rearrange("p (j c) -> p j c", j=4)),
                        waits=[f"b1tr_{g}_{i}_{gj}"], sig=("v", f"b1ev_{g}_{i}_{gj}"))
            for rnd in range(16):
                wsb = bwsyn[:, (rnd % 2) * NF:(rnd % 2) * NF + NF]
                lw = [f"b2ev_{g}_{rnd - 2}_1", f"b2mm_{g}_{rnd - 1}_0"] if rnd >= 2 else (
                    [f"b2ev_{g - 1}_{rnd + 14}_1", f"b3hid_{g - 1}_1"] +
                    ([f"b2mm_{g}_{rnd - 1}_0"] if rnd else [f"b2mm_{g - 1}_15_0"]) if g
                    else [f"qs_w_{DC - 1}", "a_done"])
                dma(SY, wsb.rearrange("p (dc n) -> p dc n", dc=DC),
                    wsynq_h[:, :, rnd * 512:(rnd + 1) * 512], f"b2w_{g}_{rnd}",
                    waits=lw)
                for i in range(2):
                    bank = (rnd % 2) * 1024 + i * 512
                    tw = ["ones2", "bbsyn", f"b2w_{g}_{rnd}",
                          f"b1ev_{g}_1_3"]
                    if rnd >= 2:
                        tw.append(f"b2ev_{g}_{rnd - 2}_{i}")
                    elif g:
                        tw.append(f"b2ev_{g - 1}_{rnd + 14}_{i}")
                    op(T, lambda e, bank=bank, rnd=rnd: e.matmul(
                        ps[:, bank:bank + 512], ones2[:],
                        bbsyn[:, rnd * 512:(rnd + 1) * 512],
                        start=True, stop=False), waits=tw)
                    for dcx in range(DC):
                        op(T, lambda e, bank=bank, dcx=dcx, i=i, wsb=wsb: e.matmul(
                            ps[:, bank:bank + 512],
                            bx2lnT.rearrange("p (dc t) -> p dc t", dc=DC)
                            [:, dcx, i * 128:(i + 1) * 128],
                            wsb.rearrange("p (dc n) -> p dc n", dc=DC)[:, dcx, :],
                            start=False, stop=(dcx == DC - 1)),
                            sig=("t", f"b2mm_{g}_{rnd}_{i}") if dcx == DC - 1 else None)
                    eng, sname = (V, "v") if (rnd + i) % 2 == 0 else (SC, "s")
                    if eng == V:
                        op(V, lambda e, bank=bank, i=i, rnd=rnd: e.tensor_scalar(
                            zA[:, i * NF + rnd * 512:i * NF + (rnd + 1) * 512],
                            ps[:, bank:bank + 512], scal[:, 4:5], consts[:, 2:3],
                            op0=ALU.mult, op1=ALU.add),
                            waits=[f"b2mm_{g}_{rnd}_{i}"],
                            sig=("v", f"b2ev_{g}_{rnd}_{i}"))
                    else:
                        op(SC, lambda e, bank=bank, i=i, rnd=rnd: e.activation(
                            zA[:, i * NF + rnd * 512:i * NF + (rnd + 1) * 512],
                            ps[:, bank:bank + 512], AF.Identity, bias=consts[:, 2:3],
                            scale=scal[:, 4:5]),
                            waits=[f"b2mm_{g}_{rnd}_{i}"],
                            sig=("s", f"b2ev_{g}_{rnd}_{i}"))
            for i in range(2):
                zt = zA[:, i * NF:(i + 1) * NF]
                zdone = [f"b2ev_{g}_{r}_{i}" for r in (14, 15)]
                op(V, lambda e, zt=zt: e.tensor_scalar(
                    cdump[:, 0:NF], zt, 1.0, None, op0=ALU.mult, op1=ALU.add,
                    accum_out=stt[:, 12 + i:13 + i]),
                    waits=zdone,
                    sig=("v", f"b3sum_{g}_{i}"))
                op(SC, lambda e, zt=zt: e.activation(
                    dumpS2[:, 0:NF], zt, AF.Square, accum_out=stt[:, 14 + i:15 + i]),
                    waits=zdone, sig=("s", f"b3sq_{g}_{i}"))
            op(SC, lambda e: e.nop(), waits=[f"b3sq_{g}_1"], sig=("s", f"b3sq2_{g}"))
            op(V, lambda e: e.tensor_scalar(stt[:, 12:14], stt[:, 12:14], 1.0 / NF,
                                            None, op0=ALU.mult),
               waits=[f"b3sum_{g}_0", f"b3sum_{g}_1", f"b3sq2_{g}"])
            op(V, lambda e: e.tensor_scalar(stt[:, 16:18], stt[:, 14:16], 1.0 / NF,
                                            None, op0=ALU.mult))
            op(V, lambda e: e.scalar_tensor_tensor(
                stt[:, 18:20], stt[:, 12:14], -1.0, stt[:, 12:14],
                op0=ALU.mult, op1=ALU.mult))
            op(V, lambda e: e.tensor_tensor(stt[:, 16:18], stt[:, 16:18], stt[:, 18:20],
                                            op=ALU.add), sig=("v", f"b3var_{g}"))
            op(SC, lambda e: e.activation(stt[:, 14:16], stt[:, 16:18], AF.Sqrt,
                                          bias=consts[:, 0:1]),
               waits=[f"b3var_{g}"], sig=("s", f"b3sg_{g}"))
            op(V, lambda e: e.scalar_tensor_tensor(
                stt[:, 0:2], stt[:, 14:16], -8.0, stt[:, 12:14],
                op0=ALU.mult, op1=ALU.add), waits=[f"b3sg_{g}"])
            op(V, lambda e: e.scalar_tensor_tensor(
                stt[:, 4:6], stt[:, 14:16], 8.0, stt[:, 12:14],
                op0=ALU.mult, op1=ALU.add))
            op(V, lambda e: e.memset(stt[:, 2:4], float(NF)))
            op(V, lambda e: e.memset(stt[:, 6:8], 0.0), sig=("v", f"b3init_{g}"))
            for it in range(NIT):
                if it < 3:
                    op(V, lambda e, it=it: e.scalar_tensor_tensor(
                        stt[:, 8:10], stt[:, 14:16], PCOEF[it], stt[:, 12:14],
                        op0=ALU.mult, op1=ALU.add),
                        waits=([f"b3init_{g}"] if it == 0 else [f"b3up_{g}_{it - 1}"]) +
                              ([f"dbgit_{it - 1}"] if _DBG and g == 0 and it else []))
                    op(V, lambda e: e.nop(), sig=("v", f"b3t_{g}_{it}"))
                else:
                    op(V, lambda e: e.tensor_tensor(stt[:, 18:20], stt[:, 4:6],
                                                    stt[:, 0:2], op=ALU.subtract),
                       waits=[f"b3up_{g}_{it - 1}"] +
                             ([f"dbgit_{it - 1}"] if _DBG and g == 0 else []))
                    op(V, lambda e: e.tensor_tensor(stt[:, 20:22], stt[:, 6:8],
                                                    stt[:, 2:4], op=ALU.subtract))
                    op(V, lambda e: e.reciprocal(stt[:, 20:22], stt[:, 20:22]))
                    op(V, lambda e: e.tensor_tensor(stt[:, 18:20], stt[:, 18:20],
                                                    stt[:, 20:22], op=ALU.mult))
                    op(V, lambda e: e.tensor_scalar(stt[:, 20:22], stt[:, 6:8], KTGT,
                                                    None, op0=ALU.subtract))
                    op(V, lambda e: e.tensor_tensor(stt[:, 20:22], stt[:, 20:22],
                                                    stt[:, 18:20], op=ALU.mult))
                    op(V, lambda e: e.tensor_tensor(stt[:, 8:10], stt[:, 4:6],
                                                    stt[:, 20:22], op=ALU.subtract))
                    op(V, lambda e: e.tensor_tensor(stt[:, 18:20], stt[:, 0:2],
                                                    stt[:, 4:6], op=ALU.add))
                    op(V, lambda e: e.tensor_scalar(stt[:, 18:20], stt[:, 18:20], 0.5,
                                                    None, op0=ALU.mult))
                    op(V, lambda e: e.scalar_tensor_tensor(
                        stt[:, 20:22], stt[:, 8:10], 1.0, stt[:, 0:2],
                        op0=ALU.mult, op1=ALU.is_gt))
                    op(V, lambda e: e.scalar_tensor_tensor(
                        stt[:, 22:24], stt[:, 8:10], 1.0, stt[:, 4:6],
                        op0=ALU.mult, op1=ALU.is_lt))
                    op(V, lambda e: e.tensor_tensor(stt[:, 20:22], stt[:, 20:22],
                                                    stt[:, 22:24], op=ALU.mult))
                    op(V, lambda e: e.tensor_tensor(stt[:, 22:24], stt[:, 8:10],
                                                    stt[:, 18:20], op=ALU.subtract))
                    op(V, lambda e: e.tensor_tensor(stt[:, 22:24], stt[:, 22:24],
                                                    stt[:, 20:22], op=ALU.mult))
                    op(V, lambda e: e.tensor_tensor(stt[:, 8:10], stt[:, 18:20],
                                                    stt[:, 22:24], op=ALU.add),
                       sig=("v", f"b3t_{g}_{it}"))
                op(V, lambda e: e.tensor_scalar(
                    cdump[:, 0:NF], zA[:, 0:NF], stt[:, 8:9], None,
                    op0=ALU.is_ge, op1=ALU.add, accum_out=stt[:, 10:11]),
                    waits=[f"b3t_{g}_{it}"], sig=("v", f"b3c0_{g}_{it}"))
                op(V, lambda e: e.tensor_scalar(stt[:, 24:25], stt[:, 9:10], -1.0,
                                                None, op0=ALU.mult),
                   sig=("v", f"b3nt_{g}_{it}"))
                op(SC, lambda e: e.activation(
                    dumpS2[:, 0:NF], zA[:, NF:2 * NF], AF.Sign, bias=stt[:, 24:25],
                    accum_out=stt[:, 25:26]),
                    waits=[f"b3nt_{g}_{it}"], sig=("s", f"b3c1_{g}_{it}"))
                op(V, lambda e: e.tensor_scalar(
                    stt[:, 11:12], stt[:, 25:26], float(NF), 0.5,
                    op0=ALU.add, op1=ALU.mult), waits=[f"b3c1_{g}_{it}"])
                op(V, lambda e: e.tensor_scalar(mint[:, 0:2], stt[:, 10:12],
                                                float(KTOP), None, op0=ALU.is_ge),
                   waits=[f"b3c0_{g}_{it}"])
                op(V, lambda e: e.tensor_scalar(mint[:, 2:4], stt[:, 10:12],
                                                float(KTOP), None, op0=ALU.is_lt))
                op(V, lambda e: e.select(stt[:, 0:2], mint[:, 0:2], stt[:, 8:10],
                                         stt[:, 0:2]))
                op(V, lambda e: e.select(stt[:, 2:4], mint[:, 0:2], stt[:, 10:12],
                                         stt[:, 2:4]))
                op(V, lambda e: e.select(stt[:, 4:6], mint[:, 2:4], stt[:, 8:10],
                                         stt[:, 4:6]))
                op(V, lambda e: e.select(stt[:, 6:8], mint[:, 2:4], stt[:, 10:12],
                                         stt[:, 6:8]), sig=("v", f"b3up_{g}_{it}"))
                if _DBG and g == 0:
                    dma(SY, dbgit_h[it], stt[:, 0:16], f"dbgit_{it}",
                        waits=[f"b3up_{g}_{it}"])
            if _DBG and g == 0:
                dma(SY, dbgz_h, zA[:, 0:2 * NF], "dbgz",
                    waits=[f"b2ev_{g}_15_0", f"b2ev_{g}_15_1", f"b3up_{g}_{NIT - 1}"])
            for i in range(2):
                zt = zA[:, i * NF:(i + 1) * NF]
                op(V, lambda e, zt=zt, i=i: e.scalar_tensor_tensor(
                    w_scr[:, 0:NF], zt, stt[:, 4 + i:5 + i], zt,
                    op0=ALU.is_lt, op1=ALU.mult),
                    waits=[f"b3up_{g}_{NIT - 1}"] + ([f"b3hid_{g}_{i - 1}"] if i else []),
                    sig=("v", f"b3w1_{g}_{i}"))
                op(V, lambda e: e.max(out=fin[:, 0:8], in_=w_scr[:, 0:NF]),
                   waits=[f"b3w1_{g}_{i}"], sig=("v", f"b3m8a_{g}_{i}"))
                op(V, lambda e: e.scalar_tensor_tensor(
                    w_scr[:, 0:NF], w_scr[:, 0:NF], fin[:, 7:8], w_scr[:, 0:NF],
                    op0=ALU.is_lt, op1=ALU.mult), waits=[f"b3m8a_{g}_{i}"])
                op(V, lambda e: e.max(out=fin[:, 8:16], in_=w_scr[:, 0:NF]),
                   sig=("v", f"b3m8b_{g}_{i}"))
                op(V, lambda e, i=i: e.tensor_scalar(
                    fin[:, 16:17], stt[:, 6 + i:7 + i], float(KTOP - 1), -1.0,
                    op0=ALU.subtract, op1=ALU.mult))
                op(V, lambda e: e.tensor_scalar(fin[:, 23:24], fin[:, 16:17], 0.75,
                                                None, op0=ALU.subtract))
                op(V, lambda e: e.tensor_scalar(fin[:, 24:25], fin[:, 16:17], 0.25,
                                                None, op0=ALU.add))
                op(V, lambda e: e.tensor_scalar(small[:, 0:16], jrow[:], fin[:, 23:24],
                                                None, op0=ALU.is_gt),
                   waits=[f"b3m8b_{g}_{i}"])
                op(V, lambda e: e.tensor_scalar(dumpV[:, 0:16], jrow[:], fin[:, 24:25],
                                                None, op0=ALU.is_lt))
                op(V, lambda e: e.tensor_tensor(small[:, 0:16], small[:, 0:16],
                                                dumpV[:, 0:16], op=ALU.mult))
                op(V, lambda e: e.tensor_tensor(small[:, 0:16], small[:, 0:16],
                                                fin[:, 0:16], op=ALU.mult))
                op(V, lambda e: e.reduce_sum(fin[:, 17:18], small[:, 0:16], axis=AXX))
                op(V, lambda e: e.tensor_scalar(fin[:, 18:19], fin[:, 16:17], -0.5, 15.5,
                                                op0=ALU.is_gt, op1=ALU.is_lt))
                op(V, lambda e: e.tensor_scalar(fin[:, 19:20], fin[:, 16:17], -0.5,
                                                None, op0=ALU.is_gt))
                op(V, lambda e: e.tensor_scalar(fin[:, 20:21], fin[:, 16:17], 15.75,
                                                None, op0=ALU.is_lt))
                op(V, lambda e: e.tensor_tensor(fin[:, 19:20], fin[:, 19:20],
                                                fin[:, 20:21], op=ALU.mult))
                op(V, lambda e, i=i: e.tensor_tensor(fin[:, 21:22], fin[:, 17:18],
                                                     stt[:, 0 + i:1 + i],
                                                     op=ALU.subtract))
                op(V, lambda e: e.tensor_tensor(fin[:, 21:22], fin[:, 21:22],
                                                fin[:, 19:20], op=ALU.mult))
                op(V, lambda e, i=i: e.tensor_tensor(fin[:, 22:23], stt[:, 0 + i:1 + i],
                                                     fin[:, 21:22], op=ALU.add))
                op(V, lambda e, i=i: e.tensor_scalar(stt[:, 26 + i:27 + i],
                                                     fin[:, 22:23], SHIFT, None,
                                                     op0=ALU.max),
                   sig=("v", f"b3th_{g}_{i}"))
                if _DBG and g == 0 and i == 0:
                    dma(SY, dbgst_h, stt[:, 0:27], "dbgst",
                        waits=[f"b3th_{g}_0", f"b3th_{g}_1" if False else f"b3th_{g}_0"])
                for ncc in range(16):
                    h1w = [f"b3th_{g}_{i}"] + ([f"b3tev_{g}_{i}_{ncc - 1}"] if ncc else ([f"b3tev_{g}_{i - 1}_15"] if i else []))
                    if _DBG and g == 0 and i == 0 and ncc == 0:
                        h1w.append("dbgz")
                        h1w.append("dbgst")
                    op(V, lambda e, zt=zt, ncc=ncc, i=i: e.scalar_tensor_tensor(
                        a2yln[:, 0:512], zt[:, ncc * 512:(ncc + 1) * 512],
                        stt[:, 26 + i:27 + i], zt[:, ncc * 512:(ncc + 1) * 512],
                        op0=ALU.is_ge, op1=ALU.mult),
                        waits=h1w,
                        sig=("v", f"b3h1_{g}_{i}_{ncc}"))
                    op(V, lambda e: e.tensor_scalar(
                        a2yln[:, 1024:1536].bitcast(BF16)[:, 0:512], a2yln[:, 0:512],
                        -SHIFT, 0.0, op0=ALU.add, op1=ALU.max),
                        sig=("v", f"b3h2_{g}_{i}_{ncc}"))
                    for j in range(4):
                        tw = [f"b3h2_{g}_{i}_{ncc}"]
                        if ncc:
                            tw.append(f"b3tev_{g}_{i}_{ncc - 1}")
                        elif i:
                            tw.append(f"b3tev_{g}_0_15")
                        else:
                            tw.append(f"b1ev_{g}_1_3")
                        op(T, lambda e, j=j: e.transpose(
                            psT[:, j * 128:(j + 1) * 128],
                            a2yln[:, 1024:1536].bitcast(BF16)[:, j * 128:(j + 1) * 128],
                            ident[:]),
                            waits=tw, sig=("t", f"b3tr_{g}_{i}_{ncc}") if j == 3 else None)
                    op(V, lambda e, ncc=ncc, i=i: e.tensor_copy(
                        hidT.rearrange("p (nc t) -> p nc t", nc=NCH)
                        [:, ncc * 4:(ncc + 1) * 4, i * 128:(i + 1) * 128],
                        psT[:, 0:512].rearrange("p (j c) -> p j c", j=4)),
                        waits=[f"b3tr_{g}_{i}_{ncc}"], sig=("v", f"b3tev_{g}_{i}_{ncc}"))
                op(V, lambda e: e.nop(), waits=[f"b3tev_{g}_{i}_15"],
                   sig=("v", f"b3hid_{g}_{i}"))
                op(V, lambda e: e.nop(), waits=[f"b3hid_{g}_{i}"],
                   sig=("v", f"b3w_{g}_{i}"))
            if _DBG and g == 0:
                dma(SY, dbghid_h, hidT[:, 0:2 * NF], "dbghid",
                    waits=[f"b3hid_{g}_1"])
            for ncx in range(NCH):
                wsb = bwout[:, (ncx % 2) * 2048:(ncx % 2) * 2048 + 2048]
                lw = [f"b4ev_{g}_{0}"] if False else []
                if ncx >= 1:
                    lw = [f"b4mm_{g}_{ncx - 1}"]
                elif g:
                    lw = [f"b4mm_{g - 1}_{NCH - 1}"]
                dma(SY, wsb, woutq_h[ncx], f"b4w_{g}_{ncx}", waits=lw)
                for i in range(2):
                    for d2c in range(4):
                        bank = i * 2048 + d2c * 512
                        tw = []
                        if ncx == 0:
                            tw = ["ones2", "bbout", f"b3hid_{g}_1", f"b4w_{g}_{ncx}"]
                            if g:
                                tw.append(f"b5ev_{g - 1}_1_{d2c}")
                            op(T, lambda e, bank=bank, d2c=d2c: e.matmul(
                                ps[:, bank:bank + 512], ones2[:],
                                bbout[:, d2c * 512:(d2c + 1) * 512],
                                start=True, stop=False), waits=tw)
                        op(T, lambda e, bank=bank, ncx=ncx, i=i, d2c=d2c, wsb=wsb:
                           e.matmul(
                            ps[:, bank:bank + 512],
                            hidT.rearrange("p (nc t) -> p nc t", nc=NCH)
                            [:, ncx, i * 128:(i + 1) * 128],
                            wsb[:, d2c * 512:(d2c + 1) * 512],
                            start=False, stop=(ncx == NCH - 1)),
                            waits=[f"b4w_{g}_{ncx}"] if (i == 0 and d2c == 0 and ncx) else [],
                            sig=("t", f"b4mm_{g}_{ncx}") if i == 1 and d2c == 3 else None)
            for i in range(2):
                h2buf = bh2 if i == 0 else zA[:, 8192:10240]
                for d2c in range(4):
                    bank = i * 2048 + d2c * 512
                    op(V, lambda e, bank=bank, d2c=d2c, i=i, h2buf=h2buf: e.scalar_tensor_tensor(
                        h2buf[:, d2c * 512:(d2c + 1) * 512], ps[:, bank:bank + 512],
                        scal[:, 5:6], bx2ln[:, i * 2048 + d2c * 512:i * 2048 + (d2c + 1) * 512],
                        op0=ALU.mult, op1=ALU.add, accum_out=small[:, i * 4 + d2c:i * 4 + d2c + 1]),
                        waits=[f"b4mm_{g}_{NCH - 1}"],
                        sig=("v", f"b5ev_{g}_{i}_{d2c}"))
            if _DBG and g == 0:
                dma(SY, dbgh2_h, bh2[:, 0:2048], "dbgh2",
                    waits=[f"b5ev_{g}_0_{d}" for d in range(4)])
                dma(SY, dbgh2b_h, zA[:, 8192:10240], "dbgh2b",
                    waits=[f"b5ev_{g}_1_{d}" for d in range(4)])
            for i in range(2):
                qtt = gt0 + i
                h2buf = bh2 if i == 0 else zA[:, 8192:10240]
                op(SC, lambda e, h2buf=h2buf: e.activation(
                    dumpS[:, 0:2048], h2buf[:, 0:2048], AF.Square,
                    accum_out=small[:, 8:9]),
                   waits=[f"b5ev_{g}_{i}_{d}" for d in range(4)] +
                         ([f"b5u_{g}_{i - 1}"] if i else []) +
                         (["dbgh2", "dbgh2b"] if _DBG and g == 0 and i == 0 else []),
                   sig=("s", f"b5sq_{g}_{i}"))
                op(V, lambda e, i=i: e.reduce_sum(small[:, 9:10],
                                                  small[:, i * 4:i * 4 + 4], axis=AXX),
                   waits=[f"b5ev_{g}_{i}_{d}" for d in range(4)] +
                         ([f"b5n_{g}_{i - 1}"] if i else []))
                op(V, lambda e: e.tensor_scalar(small[:, 9:10], small[:, 9:10],
                                                1.0 / DM, None, op0=ALU.mult))
                op(V, lambda e: e.tensor_scalar(small[:, 10:11], small[:, 8:9],
                                                1.0 / DM, None, op0=ALU.mult),
                   waits=[f"b5sq_{g}_{i}"])
                op(V, lambda e: e.scalar_tensor_tensor(
                    small[:, 11:12], small[:, 9:10], -1.0, small[:, 9:10],
                    op0=ALU.mult, op1=ALU.mult))
                op(V, lambda e: e.tensor_tensor(small[:, 10:11], small[:, 10:11],
                                                small[:, 11:12], op=ALU.add),
                   sig=("v", f"b5var_{g}_{i}"))
                op(SC, lambda e: e.activation(small[:, 12:13], small[:, 10:11],
                                              AF.Sqrt, bias=consts[:, 0:1]),
                   waits=[f"b5var_{g}_{i}"], sig=("s", f"b5sd_{g}_{i}"))
                op(V, lambda e: e.reciprocal(small[:, 13:14], small[:, 12:13]),
                   waits=[f"b5sd_{g}_{i}"])
                op(V, lambda e: e.scalar_tensor_tensor(
                    small[:, 14:15], small[:, 9:10], -1.0, small[:, 13:14],
                    op0=ALU.mult, op1=ALU.mult), sig=("v", f"b5rs_{g}_{i}"))
                op(SC, lambda e, h2buf=h2buf: e.activation(
                    bu[:, 0:2048], h2buf[:, 0:2048], AF.Identity,
                    bias=small[:, 14:15], scale=small[:, 13:14]),
                   waits=[f"b5rs_{g}_{i}"] + (["dbgu_0", "dbgln2_0"] if _DBG and g == 0 and i == 1 else []),
                   sig=("s", f"b5u_{g}_{i}"))
                if _DBG and g == 0:
                    dma(SY, dbgu_h[i], bu[:, 0:2048], f"dbgu_{i}",
                        waits=[f"b5u_{g}_{i}"])
                for gj in range(4):
                    for j in range(4):
                        dcx = gj * 4 + j
                        tw = ["ident", f"b5u_{g}_{i}"]
                        if gj:
                            tw.append(f"b5gb_{g}_{i}_{gj - 1}")
                        elif i:
                            tw.append(f"b5bb_{g}_0_3")
                        else:
                            tw.append(f"b5ev_{g}_1_3")
                            tw.append(f"b3tev_{g}_1_15")
                        op(T, lambda e, dcx=dcx, j=j: e.transpose(
                            psT[:, j * 128:(j + 1) * 128],
                            bu[:, dcx * 128:(dcx + 1) * 128], ident[:]),
                            waits=tw, sig=("t", f"b5ut_{g}_{i}_{gj}") if j == 3 else None)
                    for j in range(4):
                        dcx = gj * 4 + j
                        op(V, lambda e, dcx=dcx, j=j: e.tensor_scalar(
                            bln2T[:, dcx * 128:(dcx + 1) * 128],
                            psT[:, j * 128:(j + 1) * 128],
                            g_fm[:, dcx:dcx + 1], b_fm[:, dcx:dcx + 1],
                            op0=ALU.mult, op1=ALU.add),
                            waits=[f"b5ut_{g}_{i}_{gj}", "g_fm", "b_fm"],
                            sig=("v", f"b5gb_{g}_{i}_{gj}") if j == 3 else None)
                for gj in range(4):
                    for j in range(4):
                        dcx = gj * 4 + j
                        tw = [f"b5gb_{g}_{i}_3"]
                        if gj:
                            tw.append(f"b5bb_{g}_{i}_{gj - 1}")
                        op(T, lambda e, dcx=dcx, j=j: e.transpose(
                            psT[:, j * 128:(j + 1) * 128],
                            bln2T[:, dcx * 128:(dcx + 1) * 128], ident[:]),
                            waits=tw, sig=("t", f"b5tb_{g}_{i}_{gj}") if j == 3 else None)
                    op(V, lambda e, gj=gj: e.tensor_copy(
                        bln2[:, gj * 512:(gj + 1) * 512], psT[:, 0:512]),
                        waits=[f"b5tb_{g}_{i}_{gj}"], sig=("v", f"b5bb_{g}_{i}_{gj}"))
                if _DBG and g == 0:
                    dma(SY, dbgln2_h[i], bln2[:, 0:2048], f"dbgln2_{i}",
                        waits=[f"b5bb_{g}_{i}_{gj}" for gj in range(4)])
                op(V, lambda e: e.tensor_scalar(
                    cdump[:, 0:2048], bln2[:, 0:2048], 1.0, None, op0=ALU.mult,
                    op1=ALU.add, accum_out=small[:, 8:9]),
                    waits=[f"b5bb_{g}_{i}_{gj}" for gj in range(4)],
                    sig=("v", f"b5s_{g}_{i}"))
                op(SC, lambda e: e.activation(dumpS[:, 0:2048], bln2[:, 0:2048],
                                              AF.Square, accum_out=small[:, 9:10]),
                   waits=[f"b5bb_{g}_{i}_{gj}" for gj in range(4)] + [f"b5s_{g}_{i}"],
                   sig=("s", f"b5s2_{g}_{i}"))
                op(V, lambda e: e.tensor_scalar(small[:, 10:11], small[:, 8:9], 1.0 / DM,
                                                None, op0=ALU.mult),
                   waits=[f"b5s_{g}_{i}"])
                op(V, lambda e: e.tensor_scalar(small[:, 11:12], small[:, 9:10], 1.0 / DM,
                                                None, op0=ALU.mult),
                   waits=[f"b5s2_{g}_{i}"])
                op(V, lambda e: e.scalar_tensor_tensor(
                    small[:, 12:13], small[:, 10:11], -1.0, small[:, 10:11],
                    op0=ALU.mult, op1=ALU.mult))
                op(V, lambda e: e.tensor_tensor(small[:, 11:12], small[:, 11:12],
                                                small[:, 12:13], op=ALU.add),
                   sig=("v", f"b5ov_{g}_{i}"))
                op(SC, lambda e: e.activation(small[:, 13:14], small[:, 11:12], AF.Sqrt,
                                              bias=consts[:, 0:1]),
                   waits=[f"b5ov_{g}_{i}"], sig=("s", f"b5osd_{g}_{i}"))
                op(V, lambda e: e.reciprocal(small[:, 14:15], small[:, 13:14]),
                   waits=[f"b5osd_{g}_{i}"])
                op(V, lambda e: e.scalar_tensor_tensor(
                    small[:, 15:16], small[:, 10:11], -1.0, small[:, 14:15],
                    op0=ALU.mult, op1=ALU.mult), sig=("v", f"b5ors_{g}_{i}"))
                lw = [f"b5s_{g}_{i}", f"b5s2_{g}_{i}"]
                if g or i:
                    lw.append(f"bout_{qtt - 1}")
                dma(SY, bx2fin, x2_h[qtt * 128:(qtt + 1) * 128, :], f"b5x2_{g}_{i}",
                    waits=lw, qsem="lb")
                op(SC, lambda e: e.activation(bout_sb[:, 0:2048], bln2[:, 0:2048],
                                              AF.Identity, bias=small[:, 15:16],
                                              scale=small[:, 14:15]),
                   waits=[f"b5ors_{g}_{i}"] + ([f"boutw_{qtt - 1}"] if (g or i) else []),
                   sig=("s", f"b5n_{g}_{i}"))
                op(V, lambda e: e.tensor_tensor(bout_sb[:, 0:2048], bout_sb[:, 0:2048],
                                                bx2fin, op=ALU.add),
                   waits=[f"b5n_{g}_{i}", f"b5x2_{g}_{i}"], sig=("v", f"bout_{qtt}"))
                dma(SY, out_h[qtt * 128:(qtt + 1) * 128, :], bout_sb[:, 0:2048],
                    f"boutw_{qtt}", waits=[f"bout_{qtt}"], qsem="lo")
                op(V, lambda e: e.nop(), waits=[f"boutw_{qtt}"],
                   sig=("v", f"bfin_{g}_{i}"))
        if _KPH >= 7:
            op(V, lambda e: e.nop(),
               waits=[f"boutw_{q}" for q in range(TT)], sig=("v", "the_end"))

    p0 = Prog(None, None)
    program(p0)
    if _os.environ.get("KCHECK"):
        pr = Prog("REC", None)
        program(pr)
        streams = {}
        for idx, (eng, waits, sig, n, chain) in enumerate(pr.rec):
            streams.setdefault(eng, []).append((idx, waits, sig, n))
        pos = dict((e, 0) for e in streams)
        cnt = dict((k, 0) for k in sems)
        progress = True
        while progress:
            progress = False
            for e, st in streams.items():
                while pos[e] < len(st):
                    idx, waits, sig, n = st[pos[e]]
                    ok = True
                    for w in waits:
                        if isinstance(w, tuple):
                            sname, v = w
                        else:
                            sname, v = ev[w]
                        if cnt[sname] < v:
                            ok = False
                            break
                    if not ok:
                        break
                    if sig:
                        cnt[sig[0]] += n
                    pos[e] += 1
                    progress = True
        stuck = {e: (pos[e], len(st)) for e, st in streams.items() if pos[e] < len(st)}
        if stuck:
            print("DEADLOCK! blocked fronts:")
            for e, (p, L) in stuck.items():
                idx, waits, sig, n = streams[e][p]
                missing = [(w, (ev[w] if not isinstance(w, tuple) else w))
                           for w in waits
                           if (cnt[(ev[w] if not isinstance(w, tuple) else w)[0]]
                               < (ev[w] if not isinstance(w, tuple) else w)[1])]
                print(f"  eng {e} at op {p}/{L}: missing {missing[:4]}")
            raise SystemExit(1)
        print("KCHECK: no deadlock")
    with nc.Block() as block:
        @block.sync
        def _(eng):
            program(Prog(SY, {SY: eng}))

        @block.scalar
        def _(eng):
            program(Prog(SC, {SC: eng}))

        @block.vector
        def _(eng):
            program(Prog(V, {V: eng}))

        @block.gpsimd
        def _(eng):
            program(Prog(G, {G: eng}))

        @block.tensor
        def _(eng):
            program(Prog(T, {T: eng}))

    es.close()
    return nc


_NC_CACHE = None


def kernel(L0_attn_x, L0_attn_w_qkv, L0_attn_w_out, L0_ffn_W_syn, L0_ffn_b_syn,
           L0_ffn_W_out, L0_ffn_b_out, L0_ffn_gamma, L0_ffn_beta):
    global _NC_CACHE
    if _NC_CACHE is None:
        _NC_CACHE = build_program()
    nc = _NC_CACHE
    x = np.ascontiguousarray(np.asarray(L0_attn_x, dtype=np.float32))
    base = {
        "w_qkv": np.ascontiguousarray(np.asarray(L0_attn_w_qkv, np.float32)),
        "w_out": np.ascontiguousarray(np.asarray(L0_attn_w_out, np.float32)),
        "W_syn": np.ascontiguousarray(np.asarray(L0_ffn_W_syn, np.float32)),
        "b_syn": np.ascontiguousarray(np.asarray(L0_ffn_b_syn, np.float32)),
        "W_out_f": np.ascontiguousarray(np.asarray(L0_ffn_W_out, np.float32)),
        "b_out": np.ascontiguousarray(np.asarray(L0_ffn_b_out, np.float32)),
        "gamma": np.ascontiguousarray(np.asarray(L0_ffn_gamma, np.float32)),
        "beta": np.ascontiguousarray(np.asarray(L0_ffn_beta, np.float32)),
    }
    in_maps = [dict(base, x=np.ascontiguousarray(x[c])) for c in range(8)]
    res = run_bass_kernel_spmd(nc, in_maps, core_ids=list(range(8)))
    return np.stack([np.asarray(r["out"], np.float32) for r in res.results], axis=0)
